# version 01 (final)
# baseline (speedup 1.0000x reference)
import numpy as np
import concourse.bass as bass
import concourse.mybir as mybir
from concourse.bass_utils import run_bass_kernel_spmd
from contextlib import ExitStack

F32 = mybir.dt.float32
BF16 = mybir.dt.bfloat16
AF = mybir.ActivationFunctionType
ALU = mybir.AluOpType
AXX = mybir.AxisListType.X

NCORES = 8
D = 1024
CC = 8
T = 2048
IC = 16
NE = 9
H = 16
ALPHA = 1.702
EPS = 1e-5
STAGES = 3
DEBUG_DUMP = None
ADEBUG = False
ODUMP = False

RG = [[0, 1], [2, 3], [4, 5], [6, 7]]
QKV_ORDER = list(range(8, 16)) + list(range(16, 24)) + list(range(0, 8))
ABW = 36864


def _pack(w):
    return np.ascontiguousarray(w.reshape(8, 128, 128).transpose(1, 0, 2).reshape(128, 1024))


def host_prep(inp):
    f = np.float32
    I2 = 4096
    perm = np.concatenate([np.arange(0, I2, 2), np.arange(1, I2, 2)])
    w1 = inp["L0_ffn_w1"]; w3 = inp["L0_ffn_w3"]; w2 = inp["L0_ffn_w2"]
    b1 = inp["L0_ffn_b1"]; b3 = inp["L0_ffn_b3"]; b2 = inp["L0_ffn_b2"]
    sw1 = inp["L0_ffn_sw1"]; sw3 = inp["L0_ffn_sw3"]; sw2 = inp["L0_ffn_sw2"]
    sb1 = inp["L0_ffn_sb1"]; sb3 = inp["L0_ffn_sb3"]; sb2 = inp["L0_ffn_sb2"]

    WU = np.empty((NE * IC, 128, 5120), f)
    B1H = np.empty((NE, 128, 32), f)
    B3H = np.empty((NE, 128, 32), f)
    for e in range(NE):
        w1p = np.asarray(w1[e] if e < 8 else sw1)[:, perm]
        w3p = np.asarray(w3[e] if e < 8 else sw3)[:, perm]
        w2e = np.asarray(w2[e] if e < 8 else sw2)
        b1p = np.asarray(b1[e] if e < 8 else sb1)[perm]
        b3p = np.asarray(b3[e] if e < 8 else sb3)[perm]
        B1H[e] = b1p.reshape(32, 128).T
        B3H[e] = b3p.reshape(32, 128).T
        for ic in range(IC):
            g0, l0 = ic * 128, 2048 + ic * 128
            WU[e * IC + ic] = np.concatenate([
                _pack(w1p[:, g0:g0 + 128]), _pack(w1p[:, l0:l0 + 128]),
                _pack(w3p[:, g0:g0 + 128]), _pack(w3p[:, l0:l0 + 128]),
                w2e[ic * 128:(ic + 1) * 128, :]], axis=1)

    waT = np.asarray(inp["L0_attn_w_attn"]).T.astype(f)
    WA = np.stack([_pack(waT[:, fc * 128:(fc + 1) * 128]) for fc in QKV_ORDER])
    woT = np.asarray(inp["L0_attn_w_out"]).T.astype(f)
    WO = np.stack([_pack(woT[:, cc * 128:(cc + 1) * 128]) for cc in range(8)])
    GWT = np.ascontiguousarray(
        np.asarray(inp["L0_ffn_gate_w"]).T.reshape(8, 128, 8).transpose(1, 0, 2).reshape(128, 64)).astype(f)
    B2E = np.concatenate([np.asarray(b2), np.asarray(sb2)[None]], axis=0).astype(f)

    xflat = np.ascontiguousarray(np.asarray(inp["L0_attn_x"]).reshape(-1, D))
    xcores = [np.ascontiguousarray(xflat[c * T:(c + 1) * T].T) for c in range(NCORES)]
    shared = dict(WU=WU, WA=WA, WO=WO, GWT=GWT, GB=np.asarray(inp["L0_ffn_gate_b"]).astype(f),
                  B1H=B1H, B3H=B3H, B2E=B2E)
    return shared, xcores


ENGINES = ("tensor", "scalar", "vector", "gpsimd", "sync")
ESEM = {"tensor": "pe", "scalar": "act", "vector": "dve", "gpsimd": "gq", "sync": "sq"}


class Plan:
    def __init__(self):
        self.ops = {e: [] for e in ENGINES}
        self.cnt = {s: 0 for s in ESEM.values()}
        self.reg = {}
        self.lastwait = {e: {} for e in ENGINES}

    def emit(self, eng, fn, reads=(), writes=(), n_inc=1, extra_waits=()):
        sem = ESEM[eng]
        waits = {}

        def addw(s, c):
            if c > 0 and waits.get(s, 0) < c:
                waits[s] = c

        for k in reads:
            r = self.reg.get(k)
            if r and r["w"]:
                addw(*r["w"])
        for k in writes:
            r = self.reg.get(k)
            if r:
                if r["w"]:
                    addw(*r["w"])
                for rr in r["r"]:
                    addw(*rr)
        for s, c in extra_waits:
            addw(s, c)
        lw = self.lastwait[eng]
        final = tuple((s, c) for s, c in waits.items() if lw.get(s, 0) < c)
        for s, c in final:
            lw[s] = c
        self.cnt[sem] += n_inc
        my = (sem, self.cnt[sem])
        self.ops[eng].append((fn, final, n_inc, my))
        if not hasattr(self, "dbg"):
            self.dbg = {e: [] for e in ENGINES}
        self.dbg[eng].append((tuple(reads), tuple(writes), final, my))
        for k in writes:
            self.reg[k] = {"w": my, "r": []}
        for k in reads:
            self.reg.setdefault(k, {"w": None, "r": []})["r"].append(my)
        return my


_LASTPLAN = [None]


def build_nc(stages=STAGES):
    nc = bass.Bass("TRN2", target_bir_lowering=False)

    x_in = nc.dram_tensor("x_in", [D, T], F32, kind="ExternalInput")
    WU = nc.dram_tensor("WU", [NE * IC, 128, 5120], F32, kind="ExternalInput")
    WAd = nc.dram_tensor("WA", [24, 128, 1024], F32, kind="ExternalInput")
    WOd = nc.dram_tensor("WO", [8, 128, 1024], F32, kind="ExternalInput")
    GWT = nc.dram_tensor("GWT", [128, 64], F32, kind="ExternalInput")
    GB = nc.dram_tensor("GB", [8], F32, kind="ExternalInput")
    B1H = nc.dram_tensor("B1H", [NE, 128, 32], F32, kind="ExternalInput")
    B3H = nc.dram_tensor("B3H", [NE, 128, 32], F32, kind="ExternalInput")
    B2E = nc.dram_tensor("B2E", [NE, 1024], F32, kind="ExternalInput")
    y_out = nc.dram_tensor("y", [D, T], F32, kind="ExternalOutput")

    q_dram = nc.dram_tensor("q_dram", [D, T], BF16)
    kf_loc = nc.dram_tensor("kf_loc", [D, T], BF16)
    vt_loc = nc.dram_tensor("vt_loc", [T, D], BF16)
    kf_gat = nc.dram_tensor("kf_gat", [2 * D, T], BF16)
    vt_gat = nc.dram_tensor("vt_gat", [2 * T, D], BF16)

    pl = Plan()
    es = ExitStack()
    with es:
        SBUF = lambda n, s, d: es.enter_context(nc.sbuf_tensor(n, s, d))
        PSUM = lambda n: es.enter_context(nc.psum_tensor(n, [128, 512], F32))

        x_sb = SBUF("x_sb", [128, CC * T], F32)
        AB = SBUF("AB", [128, ABW], BF16)
        AFA = SBUF("AFA", [128, 8192], F32)
        B1 = SBUF("B1", [128, NE * 32], F32)
        B3 = SBUF("B3", [128, NE * 32], F32)
        b2e = SBUF("b2e", [9, 1024], F32)
        gwt = SBUF("gwt", [128, 64], F32)
        gbrow = SBUF("gbrow", [1, 8], F32)
        gbb = SBUF("gbb", [128, 8], F32)
        ones1 = SBUF("ones1", [1, 128], F32)
        ones_al = SBUF("ones_al", [1, 128], F32)
        onesc = SBUF("onesc", [128, 1], F32)
        ident = SBUF("ident", [128, 128], F32)
        z_all = SBUF("z_all", [128, 128], F32)
        t1b = SBUF("t1b", [128, 128], F32)
        t2b = SBUF("t2b", [128, 128], F32)
        zmax = SBUF("zmax", [128, 16], F32)
        zsum = SBUF("zsum", [128, 16], F32)
        comb_sb = SBUF("comb_sb", [128, 128], F32)
        comb_ext = SBUF("comb_ext", [9, 2048], F32)
        m8t = SBUF("m8t", [128, 8], F32)
        negm8 = SBUF("negm8", [128, 16], F32)
        ones1b = SBUF("ones1b", [1, 128], BF16)
        negrow = SBUF("negrow", [1, 2048], BF16)
        ps = [PSUM(f"ps{i}") for i in range(8)]

        semnames = list(ESEM.values()) + ["cc1", "cc2"]
        sems = {s: es.enter_context(nc.semaphore(s)) for s in semnames}

        def ab(p0, p1, c0, cols):
            g0, g1 = c0 // 512, (c0 + cols - 1) // 512
            return AB[p0:p1, c0:c0 + cols], [("B", g) for g in range(g0, g1 + 1)]

        def af(p0, p1, c0, cols):
            g0, g1 = c0 // 512, (c0 + cols - 1) // 512
            return AFA[p0:p1, c0:c0 + cols], [("F", g) for g in range(g0, g1 + 1)]

        XN = lambda cc, c0, cols: ab(0, 128, cc * T + c0, cols)
        HQ = lambda sl, c0, cols: ab(0, 128, 16384 + sl * T + c0, cols)
        W2Q = lambda sl, c0, cols: ab(0, 128, 24576 + sl * 1024 + c0, cols)
        WUB = lambda p: ab(0, 128, 28672 + p * 4096, 4096)
        KTMP = lambda p: ab(0, 128, 16384 + p * 2048, 2048)
        QTMP = lambda p: ab(0, 128, 20480 + p * 2048, 2048)
        VTT = lambda p: ab(0, 128, 24576 + p * 512, 512)
        VXB = lambda p: ab(0, 128, 25600 + p * 128, 128)
        WAB = lambda p: ab(0, 128, 28672 + p * 1024, 1024)
        KH = lambda: ab(0, 65, 16384, 4096)
        QA = lambda: ab(0, 65, 20480, 2048)
        PTB = lambda p: ab(0, 128, 22528 + p * 512, 512)
        VAUG = lambda: ab(0, 128, 23552, 2080)
        WOB = lambda p: ab(0, 128, 32768 + p * 1024, 1024)

        SLOT = lambda i: af(0, 128, i * 512, 512)
        CEB = lambda c0, cols: af(0, 128, 2560 + c0, cols)
        ROW = lambda i: af(0, 1, 4608 + i * 512, 512)
        RB = lambda: af(0, 64, 0, 512)
        MT = lambda: af(0, 16, 512, 128)

        xv = lambda cc, c0, cols: x_sb[:, cc * T + c0:cc * T + c0 + cols]
        E = pl.emit

        stream = []
        ffn_par = [0]

        def ffn_unit_el(ui):
            par = ffn_par[0] % 2; ffn_par[0] += 1
            buf, bk = WUB(par)
            sl = (ui % IC) % 4
            w2v, w2k = W2Q(sl, 0, 1024)
            return dict(fn=lambda g, sem, buf=buf, ui=ui: g.dma_start(out=buf, in_=WU[ui, :, 0:4096]),
                        writes=bk, n_inc=16, buf=buf, wk=bk, w2k=w2k, w2v=w2v, ui=ui)

        def wa_el(j):
            buf, bk = WAB(j % 2)
            return dict(fn=lambda g, sem, j=j, buf=buf: g.dma_start(out=buf, in_=WAd[j]),
                        writes=bk, n_inc=16, buf=buf, wk=bk)

        def wo_el(j):
            buf, bk = WOB(j % 2)
            return dict(fn=lambda g, sem, j=j, buf=buf: g.dma_start(out=buf, in_=WOd[j]),
                        writes=bk, n_inc=16, buf=buf, wk=bk)

        for ui in range(144):
            stream.append(ffn_unit_el(ui))
        if stages >= 2:
            for j in range(24):
                stream.append(wa_el(j))
            for j in range(8):
                stream.append(wo_el(j))
        if stages >= 3:
            for ui in range(144):
                stream.append(ffn_unit_el(ui))

        nstream = [0]

        def kick(upto):
            while nstream[0] <= min(upto, len(stream) - 1):
                el = stream[nstream[0]]
                E("gpsimd", el["fn"], reads=(), writes=el["writes"], n_inc=el["n_inc"])
                nstream[0] += 1

        def init():
            def loads(sy, sem):
                for cc in range(8):
                    sy.dma_start(out=xv(cc, 0, T), in_=x_in[cc * 128:(cc + 1) * 128, :]).then_inc(sem, 16)
            E("sync", loads, writes=[("x", cc, tt) for cc in range(8) for tt in range(4)], n_inc=128)

            def small(sy, sem):
                sy.dma_start(out=B1[:], in_=B1H[:].rearrange("e p j -> p e j")).then_inc(sem, 16)
                sy.dma_start(out=B3[:], in_=B3H[:].rearrange("e p j -> p e j")).then_inc(sem, 16)
                sy.dma_start(out=b2e[:], in_=B2E[:]).then_inc(sem, 16)
                sy.dma_start(out=gwt[:], in_=GWT[:]).then_inc(sem, 16)
                sy.dma_start(out=gbrow[:], in_=bass.AP(GB, 0, [[1, 1], [1, 8]])).then_inc(sem, 16)
            E("sync", small, writes=["B1", "B3", "b2e", "gwt", "gbrow"], n_inc=80)

            E("gpsimd", lambda g, s: g.memset(ones1[:], 1.0), writes=["ones1"])
            E("gpsimd", lambda g, s: g.memset(ones1b[:], 1.0), writes=["ones1b"])
            E("gpsimd", lambda g, s: g.memset(ones_al[:], 1.0 / ALPHA), writes=["ones_al"])
            E("gpsimd", lambda g, s: g.memset(onesc[:], 1.0), writes=["onesc"])
            r5, r5k = ROW(5)
            E("gpsimd", lambda g, s: g.memset(r5, 1.0), writes=r5k)
            E("gpsimd", lambda g, s: g.iota(t1b[:], [[1, 128]], channel_multiplier=-1,
                                            allow_small_or_imprecise_dtypes=True), writes=["t1b"])
            E("gpsimd", lambda g, s: g.tensor_single_scalar(ident[:], t1b[:], 0.0, ALU.is_equal),
              reads=["t1b"], writes=["ident"])
            E("tensor", lambda t, s: t.matmul(ps[6][:, 0:8], lhsT=ones1[0:1, 0:128],
                                              rhs=gbrow[:], start=True, stop=True),
              reads=["ones1", "gbrow"], writes=["p6"])
            E("vector", lambda v, s: v.tensor_copy(gbb[:], ps[6][:, 0:8]),
              reads=["p6"], writes=["gbb"])

            def c8(sy, sem):
                for tt in range(4):
                    sy.dma_start(out=comb_ext[8:9, tt * 512:(tt + 1) * 512], in_=r5).then_inc(sem, 16)
            E("sync", c8, reads=r5k, writes=["combx8"], n_inc=64)

        def layer_norm(gating):
            sq0, sq0k = SLOT(5)
            sq1, sq1k = SLOT(6)
            for tt in range(4):
                t0 = tt * 512

                def s1fn(t, s, t0=t0):
                    last = None
                    for cc in range(8):
                        last = t.matmul(ps[4][0:1, :], lhsT=onesc[:], rhs=xv(cc, t0, 512),
                                        start=(cc == 0), stop=(cc == 7))
                    return last
                E("tensor", s1fn, reads=["onesc"] + [("x", cc, tt) for cc in range(8)], writes=["p4"])
                for cc in range(8):
                    sq, sqk = (sq0, sq0k) if cc % 2 == 0 else (sq1, sq1k)
                    E("scalar", lambda a, s, cc=cc, sq=sq, t0=t0: a.activation(
                        sq, xv(cc, t0, 512), AF.Square), reads=[("x", cc, tt)], writes=sqk)
                    E("tensor", lambda t, s, cc=cc, sq=sq: t.matmul(
                        ps[5][0:1, :], lhsT=onesc[:], rhs=sq, start=(cc == 0), stop=(cc == 7)),
                      reads=["onesc"] + sqk, writes=["p5"])
                r0, r0k = ROW(0); r1, r1k = ROW(1); r2, r2k = ROW(2)
                r3, r3k = ROW(3); r4, r4k = ROW(4)
                E("vector", lambda v, s, r0=r0: v.tensor_scalar_mul(r0, ps[4][0:1, :], 1.0 / D),
                  reads=["p4"], writes=r0k)
                E("vector", lambda v, s, r1=r1: v.tensor_scalar_mul(r1, ps[5][0:1, :], 1.0 / D),
                  reads=["p5"], writes=r1k)
                E("vector", lambda v, s, r0=r0, r2=r2: v.tensor_mul(r2, r0, r0), reads=r0k, writes=r2k)
                E("vector", lambda v, s, r1=r1, r2=r2: v.tensor_sub(r1, r1, r2), reads=r1k + r2k, writes=r1k)
                E("vector", lambda v, s, r1=r1: v.tensor_scalar_add(r1, r1, EPS),
                  reads=r1k, writes=r1k)
                E("scalar", lambda a, s, r1=r1, r2=r2: a.activation(r2, r1, AF.Sqrt),
                  reads=r1k, writes=r2k)
                E("vector", lambda v, s, r2=r2, r3=r3: v.reciprocal(r3, r2), reads=r2k, writes=r3k)
                E("vector", lambda v, s, r0=r0, r3=r3, r4=r4: v.scalar_tensor_tensor(
                    r4, r0, -1.0, r3, ALU.mult, ALU.mult), reads=r0k + r3k, writes=r4k)
                E("tensor", lambda t, s, r3=r3: t.matmul(
                    ps[6][:], lhsT=ones1[0:1, 0:128], rhs=r3, start=True, stop=True),
                  reads=["ones1"] + r3k, writes=["p6"])
                E("tensor", lambda t, s, r4=r4: t.matmul(
                    ps[7][:], lhsT=ones1[0:1, 0:128], rhs=r4, start=True, stop=True),
                  reads=["ones1"] + r4k, writes=["p7"])
                for cc in range(8):
                    xf, xfk = SLOT(cc) if gating else SLOT(cc % 2)
                    xnv, xnk = XN(cc, t0, 512)
                    E("vector", lambda v, s, cc=cc, xf=xf, t0=t0: v.tensor_mul(
                        xf, xv(cc, t0, 512), ps[6][:]), reads=[("x", cc, tt), "p6"], writes=xfk)
                    E("vector", lambda v, s, xf=xf, xnv=xnv: v.tensor_add(xnv, xf, ps[7][:]),
                      reads=xfk + ["p7"], writes=xnk)
                    if gating:
                        E("vector", lambda v, s, xf=xf: v.tensor_add(xf, xf, ps[7][:]),
                          reads=xfk + ["p7"], writes=xfk)
                if gating:
                    allslotk = []
                    for cc in range(8):
                        allslotk += SLOT(cc)[1]
                    for st in range(4):
                        tile = tt * 4 + st
                        zp = 4 + st % 2

                        def zfn(t, s, st=st, zp=zp):
                            last = None
                            for cc in range(8):
                                xf, _ = SLOT(cc)
                                last = t.matmul(ps[zp][:, 0:8], lhsT=xf[:, st * 128:(st + 1) * 128],
                                                rhs=gwt[:, cc * 8:(cc + 1) * 8],
                                                start=(cc == 0), stop=(cc == 7))
                            return last
                        E("tensor", zfn, reads=allslotk + ["gwt"], writes=[f"p{zp}"])
                        E("scalar", lambda a, s, tile=tile, zp=zp: a.activation(
                            z_all[:, tile * 8:(tile + 1) * 8], ps[zp][:, 0:8], AF.Identity),
                          reads=[f"p{zp}"], writes=[("z", tile)])

        def seg3(apx):
            return apx.rearrange("p (s e) -> p s e", e=8)

        def gating_top2():
            zk = [("z", t) for t in range(16)]
            bc = lambda m: m[:].unsqueeze(2).to_broadcast([128, 16, 8])
            E("vector", lambda v, s: v.tensor_reduce(zmax[:], seg3(z_all[:]), AXX, ALU.max),
              reads=zk, writes=["zmax"])
            E("vector", lambda v, s: v.tensor_tensor(seg3(t1b[:]), seg3(z_all[:]), bc(zmax), ALU.subtract),
              reads=zk + ["zmax"], writes=["t1b"])
            E("scalar", lambda a, s: a.activation(t2b[:], t1b[:], AF.Exp),
              reads=["t1b"], writes=["t2b"])
            E("vector", lambda v, s: v.tensor_reduce(zsum[:], seg3(t2b[:]), AXX, ALU.add),
              reads=["t2b"], writes=["zsum"])
            E("vector", lambda v, s: v.reciprocal(zsum[:], zsum[:]), reads=["zsum"], writes=["zsum"])
            E("vector", lambda v, s: v.tensor_tensor(seg3(t1b[:]), seg3(t2b[:]), bc(zsum), ALU.mult),
              reads=["t2b", "zsum"], writes=["t1b"])
            gbc = gbb[:].unsqueeze(1).to_broadcast([128, 16, 8])
            E("vector", lambda v, s: v.tensor_tensor(seg3(t2b[:]), seg3(t1b[:]), gbc, ALU.add),
              reads=["t1b", "gbb"], writes=["t2b"])
            E("vector", lambda v, s: v.tensor_reduce(zmax[:], seg3(t2b[:]), AXX, ALU.max),
              reads=["t2b"], writes=["zmax"])
            E("vector", lambda v, s: v.tensor_tensor(seg3(z_all[:]), seg3(t2b[:]), bc(zmax), ALU.is_ge),
              reads=["t2b", "zmax"] + zk, writes=["mask1"])
            E("vector", lambda v, s: v.scalar_tensor_tensor(t2b[:], z_all[:], -1e9, t2b[:], ALU.mult, ALU.add),
              reads=["mask1", "t2b"], writes=["t2b"])
            E("vector", lambda v, s: v.tensor_reduce(zmax[:], seg3(t2b[:]), AXX, ALU.max),
              reads=["t2b"], writes=["zmax"])
            E("vector", lambda v, s: v.tensor_tensor(seg3(t2b[:]), seg3(t2b[:]), bc(zmax), ALU.is_ge),
              reads=["t2b", "zmax"], writes=["t2b"])
            E("vector", lambda v, s: v.tensor_tensor(t2b[:], t2b[:], z_all[:], ALU.add),
              reads=["t2b", "mask1"], writes=["t2b"])
            E("vector", lambda v, s: v.tensor_tensor(comb_sb[:], t1b[:], t2b[:], ALU.mult),
              reads=["t1b", "t2b"], writes=["comb_sb"])
            for tile in range(16):
                E("tensor", lambda t, s, tile=tile: t.transpose(
                    ps[6][0:8, 0:128], comb_sb[:, tile * 8:(tile + 1) * 8], ident[:]),
                  reads=["comb_sb", "ident"], writes=["p6"])
                E("vector", lambda v, s, tile=tile: v.tensor_copy(
                    comb_ext[0:8, tile * 128:(tile + 1) * 128], ps[6][0:8, 0:128]),
                  reads=["p6"], writes=[("ce", tile)])

        def bias_init():
            cek = [("ce", t) for t in range(16)] + ["combx8"]
            for tt in range(4):
                for dc in range(8):
                    bp = 4 + dc % 2
                    E("tensor", lambda t, s, tt=tt, dc=dc, bp=bp: t.matmul(
                        ps[bp][:], lhsT=b2e[0:9, dc * 128:(dc + 1) * 128],
                        rhs=comb_ext[0:9, tt * 512:(tt + 1) * 512], start=True, stop=True),
                      reads=cek + ["b2e"], writes=[f"p{bp}"])
                    E("vector", lambda v, s, tt=tt, dc=dc, bp=bp: v.scalar_tensor_tensor(
                        xv(dc, tt * 512, 512), ps[bp][:], 0.5, xv(dc, tt * 512, 512),
                        ALU.mult, ALU.add),
                      reads=[f"p{bp}", ("x", dc, tt)], writes=[("x", dc, tt)])

        def experts(base_el):
            for e in range(NE):
                if e < 8:
                    for tt in range(4):
                        r6, r6k = ROW(6)
                        E("sync", lambda sy, s, e=e, tt=tt, r6=r6: sy.dma_start(
                            out=r6, in_=comb_ext[e:e + 1, tt * 512:(tt + 1) * 512]),
                          reads=[("ce", tt * 4 + q) for q in range(4)], writes=r6k, n_inc=16)
                        E("tensor", lambda t, s, r6=r6: t.matmul(
                            ps[6][:], lhsT=ones_al[0:1, 0:128], rhs=r6, start=True, stop=True),
                          reads=["ones_al"] + r6k, writes=["p6"])
                        cebv, cebk = CEB(tt * 512, 512)
                        E("vector", lambda v, s, cebv=cebv: v.tensor_copy(cebv, ps[6][:]),
                          reads=["p6"], writes=cebk)
                for ic in range(IC):
                    eli = base_el + e * IC + ic
                    kick(eli + 1)
                    el = stream[eli]
                    wub = el["buf"]
                    wuk = el["wk"]
                    E("gpsimd", lambda g, s, el=el: g.dma_start(out=el["w2v"], in_=WU[el["ui"], :, 4096:5120]),
                      reads=(), writes=el["w2k"], n_inc=16)
                    sl = ic % 4
                    b1g = B1[:, e * 32 + ic:e * 32 + ic + 1]
                    b1l = B1[:, e * 32 + 16 + ic:e * 32 + 17 + ic]
                    b3g = B3[:, e * 32 + ic:e * 32 + ic + 1]
                    b3l = B3[:, e * 32 + 16 + ic:e * 32 + 17 + ic]
                    for tt in range(4):
                        t0 = tt * 512
                        pa = 4 * ((ic * 4 + tt) % 2)
                        xnk = []
                        for cc in range(8):
                            xnk += XN(cc, t0, 512)[1]

                        def pafn(t, s, wub=wub, t0=t0, pa=pa):
                            last = None
                            for ch in range(4):
                                for cc in range(8):
                                    last = t.matmul(
                                        ps[pa + ch][:],
                                        lhsT=wub[:, ch * 1024 + cc * 128:ch * 1024 + (cc + 1) * 128],
                                        rhs=AB[:, cc * T + t0:cc * T + t0 + 512],
                                        start=(cc == 0), stop=(cc == 7))
                            return last
                        E("tensor", pafn, reads=wuk + xnk,
                          writes=[f"p{pa}", f"p{pa+1}", f"p{pa+2}", f"p{pa+3}"])
                        s0, s0k = SLOT(0); s1, s1k = SLOT(1); s2, s2k = SLOT(2)
                        s3, s3k = SLOT(3); s4, s4k = SLOT(4)
                        E("scalar", lambda a, s, pa=pa, b1g=b1g, s0=s0: a.activation(
                            s0, ps[pa][:], AF.Identity, bias=b1g), reads=[f"p{pa}", "B1"], writes=s0k)
                        E("scalar", lambda a, s, pa=pa, b1l=b1l, s1=s1: a.activation(
                            s1, ps[pa + 1][:], AF.Identity, bias=b1l), reads=[f"p{pa+1}", "B1"], writes=s1k)
                        E("vector", lambda v, s, pa=pa, b3g=b3g, s0=s0, s2=s2: v.scalar_tensor_tensor(
                            s2, ps[pa + 2][:], b3g, s0, ALU.add, ALU.mult),
                          reads=[f"p{pa+2}", "B3"] + s0k, writes=s2k)
                        E("vector", lambda v, s, pa=pa, b3l=b3l, s1=s1, s3=s3: v.scalar_tensor_tensor(
                            s3, ps[pa + 3][:], b3l, s1, ALU.add, ALU.mult),
                          reads=[f"p{pa+3}", "B3"] + s1k, writes=s3k)
                        E("gpsimd", lambda g, s, s2=s2: g.tensor_scalar_min(s2, s2, 7.0),
                          reads=s2k, writes=s2k)
                        E("scalar", lambda a, s, s2=s2, s4=s4: a.activation(
                            s4, s2, AF.Silu, scale=ALPHA), reads=s2k, writes=s4k)
                        E("gpsimd", lambda g, s, s3=s3: g.tensor_scalar(
                            s3, s3, 7.0, -7.0, ALU.min, ALU.max), reads=s3k, writes=s3k)
                        E("vector", lambda v, s, s3=s3, s4=s4, s0=s0: v.scalar_tensor_tensor(
                            s0, s3, 1.0, s4, ALU.add, ALU.mult), reads=s3k + s4k, writes=s0k)
                        if DEBUG_DUMP == (e, ic, tt):
                            def dmp(sy, sem, s1=s1, s2=s2, s3=s3, s4=s4, s0=s0):
                                for i, sv in enumerate((s1, s2, s3, s4, s0)):
                                    sy.dma_start(out=y_out[i * 128:(i + 1) * 128, 0:512], in_=sv).then_inc(sem, 16)
                            E("sync", dmp, reads=s0k + s1k + s2k + s3k + s4k, writes=["yout"], n_inc=80)
                        hv, hk = HQ(sl, t0, 512)
                        if e < 8:
                            cebv, cebk = CEB(t0, 512)
                            E("gpsimd", lambda g, s, s0=s0, hv=hv, cebv=cebv: g.tensor_tensor(
                                hv, s0, cebv, ALU.mult), reads=s0k + cebk, writes=hk)
                        else:
                            E("gpsimd", lambda g, s, s0=s0, hv=hv: g.tensor_scalar_mul(
                                hv, s0, 1.0 / ALPHA), reads=s0k, writes=hk)
                    if sl == 3:
                        hks, w2ks = [], []
                        for q in range(4):
                            hks += HQ(q, 0, T)[1]
                            w2ks += W2Q(q, 0, 1024)[1]
                        for tt in range(4):
                            t0 = tt * 512
                            for dc in range(8):
                                bp = 6 + dc % 2

                                def pbfn(t, s, t0=t0, dc=dc, bp=bp):
                                    last = None
                                    for q in range(4):
                                        w2v = AB[:, 24576 + q * 1024 + dc * 128:
                                                 24576 + q * 1024 + (dc + 1) * 128]
                                        hq = AB[:, 16384 + q * T + t0:16384 + q * T + t0 + 512]
                                        last = t.matmul(ps[bp][:], lhsT=w2v, rhs=hq,
                                                        start=(q == 0), stop=(q == 3))
                                    return last
                                E("tensor", pbfn, reads=hks + w2ks, writes=[f"p{bp}"])
                                E("vector", lambda v, s, t0=t0, dc=dc, tt=tt, bp=bp: v.scalar_tensor_tensor(
                                    xv(dc, t0, 512), ps[bp][:], 0.5, xv(dc, t0, 512),
                                    ALU.mult, ALU.add),
                                  reads=[f"p{bp}", ("x", dc, tt)], writes=[("x", dc, tt)])

        def ffn_pass(base_el):
            layer_norm(gating=True)
            gating_top2()
            bias_init()
            experts(base_el)

        def attention():
            layer_norm(gating=False)
            if ADEBUG:
                E("gpsimd", lambda g, s: g.dma_start(out=y_out[560:688, 0:512], in_=AB[0:128, 0:512]),
                  reads=XN(0, 0, 512)[1], writes=["yout"], n_inc=16)
                E("sync", lambda sy, s: sy.dma_start(out=y_out[688:816, 0:512], in_=x_sb[:, 0:512]),
                  reads=[("x", 0, 0)], writes=["yout2"], n_inc=16)
            for j, fc in enumerate(QKV_ORDER):
                eli = 144 + j
                kick(eli + 1)
                el = stream[eli]
                wab = el["buf"]; wak = el["wk"]
                if fc < 8:
                    dstv, dstk_all = QTMP(j % 2)
                elif fc < 16:
                    dstv, dstk_all = KTMP(j % 2)
                for tt in range(4):
                    t0 = tt * 512
                    qp = tt % 2
                    xnk = []
                    for cc in range(8):
                        xnk += XN(cc, t0, 512)[1]

                    def qkfn(t, s, wab=wab, t0=t0, qp=qp):
                        last = None
                        for cc in range(8):
                            last = t.matmul(ps[qp][:], lhsT=wab[:, cc * 128:(cc + 1) * 128],
                                            rhs=AB[:, cc * T + t0:cc * T + t0 + 512],
                                            start=(cc == 0), stop=(cc == 7))
                        return last
                    E("tensor", qkfn, reads=wak + xnk, writes=[f"p{qp}"])
                    if fc < 16:
                        sk = [dstk_all[tt]]
                        E("scalar", lambda a, s, dstv=dstv, t0=t0, qp=qp: a.activation(
                            dstv[:, t0:t0 + 512], ps[qp][:], AF.Identity),
                          reads=[f"p{qp}"], writes=sk)
                    else:
                        vdst, vdk = VTT(tt % 2)
                        E("scalar", lambda a, s, vdst=vdst, qp=qp: a.activation(
                            vdst, ps[qp][:], AF.Identity), reads=[f"p{qp}"], writes=vdk)
                        for jj in range(4):
                            vxb, vxk = VXB(jj % 2)
                            E("sync", lambda sy, s, vdst=vdst, vxb=vxb, jj=jj: sy.dma_start(
                                out=vxb, in_=vdst[:, jj * 128:(jj + 1) * 128], transpose=True),
                              reads=vdk, writes=vxk, n_inc=16)
                            rr, c0 = t0 + jj * 128, (fc - 16) * 128
                            E("sync", lambda sy, s, vxb=vxb, rr=rr, c0=c0: sy.dma_start(
                                out=vt_loc[rr:rr + 128, c0:c0 + 128], in_=vxb),
                              reads=vxk, writes=[("vtl", fc, tt, jj)], n_inc=16)
                if fc < 8:
                    E("sync", lambda sy, s, dstv=dstv, fc=fc: sy.dma_start(
                        out=q_dram[fc * 128:(fc + 1) * 128, :], in_=dstv),
                      reads=dstk_all, writes=[("qd", fc)], n_inc=16)
                elif fc < 16:
                    E("sync", lambda sy, s, dstv=dstv, fc=fc: sy.dma_start(
                        out=kf_loc[(fc - 8) * 128:(fc - 7) * 128, :], in_=dstv),
                      reads=dstk_all, writes=[("kfl", fc)], n_inc=16)
            E("gpsimd", lambda g, s: g.collective_compute(
                "AllGather", ALU.bypass, replica_groups=RG, ins=[kf_loc[:]],
                outs=[kf_gat[:]]).then_inc(sems["cc1"], 1),
              reads=[("kfl", fc) for fc in range(8, 16)], n_inc=0)
            pl.reg["kf_gat"] = {"w": ("cc1", 1), "r": []}
            E("gpsimd", lambda g, s: g.collective_compute(
                "AllGather", ALU.bypass, replica_groups=RG, ins=[vt_loc[:]],
                outs=[vt_gat[:]]).then_inc(sems["cc2"], 1),
              reads=[("vtl", fc, tt, jj) for fc in range(16, 24) for tt in range(4) for jj in range(4)],
              n_inc=0)
            pl.reg["vt_gat"] = {"w": ("cc2", 1), "r": []}

            khv, khk = KH()
            qav, qak = QA()
            vab, vabk = VAUG()
            E("gpsimd", lambda g, s: g.memset(
                bass.AP(AB, 23552 + 64, [[ABW, 128], [65, 32], [1, 1]]), 1.0), writes=vabk)
            pt0, pt0k = PTB(0)
            pt1, pt1k = PTB(1)
            rbv, rbk = RB()
            mtv, mtk = MT()

            for h in range(H):
                hcc, hoff = h // 2, (h % 2) * 64
                kh1k = khk[:4]
                kh2k = khk[4:]
                E("sync", lambda sy, s, h=h: sy.dma_start(
                    out=AB[0:64, 16384:18432], in_=kf_gat[h * 64:h * 64 + 64, :]),
                  reads=["kf_gat"], writes=kh1k, n_inc=16)
                E("sync", lambda sy, s, h=h: sy.dma_start(
                    out=AB[0:64, 18432:20480], in_=kf_gat[1024 + h * 64:1024 + h * 64 + 64, :]),
                  reads=["kf_gat"], writes=kh2k, n_inc=16)
                E("sync", lambda sy, s, h=h: sy.dma_start(
                    out=bass.AP(AB, 23552, [[ABW, 128], [65, 32], [1, 64]]),
                    in_=vt_gat[:, h * 64:h * 64 + 64].rearrange("(k p) d -> p k d", p=128)),
                  reads=["vt_gat"], writes=vabk[:-1], n_inc=16)
                E("sync", lambda sy, s, h=h: sy.dma_start(
                    out=AB[0:64, 20480:22528], in_=q_dram[h * 64:h * 64 + 64, :]),
                  reads=[("qd", fc) for fc in range(8)], writes=qak, n_inc=16)
                for qt in range(16):
                    for kc in range(8):
                        pp = 4 + kc % 2
                        E("tensor", lambda t, s, qt=qt, kc=kc, pp=pp: t.matmul(
                            ps[pp][:], lhsT=AB[0:64, 20480 + qt * 128:20480 + (qt + 1) * 128],
                            rhs=AB[0:64, 16384 + kc * 512:16384 + (kc + 1) * 512],
                            start=True, stop=True),
                          reads=qak + khk, writes=[f"p{pp}"])
                        E("vector", lambda v, s, kc=kc, pp=pp: v.tensor_reduce(
                            m8t[:, kc:kc + 1], ps[pp][:], AXX, ALU.max),
                          reads=[f"p{pp}"], writes=["m8t"])
                    E("vector", lambda v, s, qt=qt: v.tensor_reduce(
                        negm8[:, qt:qt + 1], m8t[:], AXX, ALU.max, negate=True),
                      reads=["m8t"], writes=["negm8"])
                E("tensor", lambda t, s: t.transpose(ps[6][0:16, 0:128], negm8[:], ident[:]),
                  reads=["negm8", "ident"], writes=["p6"])
                E("vector", lambda v, s: v.tensor_copy(mtv, ps[6][0:16, 0:128]),
                  reads=["p6"], writes=mtk)
                E("gpsimd", lambda g, s: g.dma_start(
                    out=bass.AP(negrow, 0, [[2048, 1], [128, 16], [1, 128]]),
                    in_=mtv), reads=mtk, writes=["qarow"], n_inc=16)
                for qtg in range(4):
                    q0 = qtg * 512
                    for kt in range(32):
                        sp = 1 + kt % 2

                        def stfn(t, s, kt=kt, q0=q0, sp=sp):
                            t.matmul(ps[sp][:],
                                     lhsT=AB[0:64, 16384 + kt * 128:16384 + (kt + 1) * 128],
                                     rhs=AB[0:64, 20480 + q0:20480 + q0 + 512],
                                     start=True, stop=False)
                            return t.matmul(ps[sp][:], lhsT=ones1b[:],
                                            rhs=negrow[0:1, q0:q0 + 512],
                                            start=False, stop=True)
                        E("tensor", stfn, reads=khk + qak + ["qarow", "ones1b"], writes=[f"p{sp}"])
                        if ADEBUG and h == 0 and qtg == 0 and kt == 0:
                            sdmp, sdmpk = SLOT(3)
                            E("vector", lambda v, s, sdmp=sdmp, sp=sp: v.tensor_copy(sdmp, ps[sp][:]),
                              reads=[f"p{sp}"], writes=sdmpk)
                            def stdmp(g, sem, sdmp=sdmp):
                                g.dma_start(out=y_out[0:128, 0:512], in_=sdmp).then_inc(sem, 16)
                                g.dma_start(out=y_out[128:192, 0:128], in_=AB[0:64, 16384:16512]).then_inc(sem, 16)
                                g.dma_start(out=y_out[192:256, 0:512], in_=AB[0:64, 20480:20992]).then_inc(sem, 16)
                                g.dma_start(out=y_out[256:257, 0:2048], in_=negrow[0:1, 0:2048]).then_inc(sem, 16)
                                g.dma_start(out=y_out[288:416, 0:16], in_=negm8[:]).then_inc(sem, 16)
                                g.dma_start(out=y_out[416:432, 0:128], in_=mtv).then_inc(sem, 16)
                            E("gpsimd", stdmp, reads=sdmpk + khk + qak + ["qarow", "negm8"] + mtk,
                              writes=["yout"], n_inc=96)
                        pt, ptk = (pt0, pt0k) if kt % 2 == 0 else (pt1, pt1k)
                        E("scalar", lambda a, s, pt=pt, sp=sp: a.activation(
                            pt, ps[sp][:], AF.Exp, scale=0.125), reads=[f"p{sp}"], writes=ptk)
                        E("tensor", lambda t, s, kt=kt, pt=pt: t.matmul(
                            ps[0][0:65, :], lhsT=AB[:, 23552 + kt * 65:23552 + (kt + 1) * 65],
                            rhs=pt, start=(kt == 0), stop=(kt == 31)),
                          reads=ptk + vabk, writes=["p0"])
                    onf, onfk = SLOT(2)
                    E("scalar", lambda a, s, onf=onf: a.activation(onf[0:64, :], ps[0][0:64, :], AF.Identity),
                      reads=["p0"], writes=onfk)
                    r0, r0k = ROW(0)
                    E("vector", lambda v, s, r0=r0: v.reciprocal(r0, ps[0][64:65, 0:512]),
                      reads=["p0"], writes=r0k)
                    E("tensor", lambda t, s, r0=r0: t.matmul(
                        ps[3][0:64, :], lhsT=ones1[0:1, 0:64], rhs=r0, start=True, stop=True),
                      reads=["ones1"] + r0k, writes=["p3"])
                    E("vector", lambda v, s: v.tensor_copy(rbv, ps[3][0:64, 0:512]),
                      reads=["p3"], writes=rbk)
                    ov = AB[hoff:hoff + 64, hcc * T + q0:hcc * T + q0 + 512]
                    _, ovk = XN(hcc, q0, 512)
                    E("vector", lambda v, s, ov=ov, onf=onf: v.tensor_tensor(ov, onf[0:64, :], rbv, ALU.mult),
                      reads=onfk + rbk, writes=ovk)
                    if False and ADEBUG and h == 0 and qtg == 0:
                        onum, onumk = SLOT(2)
                        E("vector", lambda v, s, onum=onum: v.tensor_copy(onum[0:65, :], ps[0][0:65, :]),
                          reads=["p0"], writes=onumk)
                        def adm(g, sem, onum=onum, hoff=hoff, hcc=hcc):
                            g.dma_start(out=y_out[480:545, 0:512], in_=onum[0:65, :]).then_inc(sem, 16)
                            g.dma_start(out=y_out[545:609, 0:512], in_=rbv).then_inc(sem, 16)
                            g.dma_start(out=y_out[0:16, 0:128], in_=mtv).then_inc(sem, 16)
                            g.dma_start(out=y_out[16:17, 0:2048], in_=AB[64:65, 20480:22528]).then_inc(sem, 16)
                            g.dma_start(out=y_out[17:18, 0:2048], in_=AB[64:65, 16384:18432]).then_inc(sem, 16)
                            g.dma_start(out=y_out[18:19, 0:512], in_=AFA[0:1, 4608:5120]).then_inc(sem, 16)
                            g.dma_start(out=y_out[32:96, 0:512], in_=AB[hoff:hoff + 64, hcc * T:hcc * T + 512]).then_inc(sem, 16)
                            g.dma_start(out=y_out[96:160, 0:512], in_=AB[0:64, 20480:20992]).then_inc(sem, 16)
                            g.dma_start(out=y_out[160:224, 0:512], in_=AB[0:64, 16384:16896]).then_inc(sem, 16)
                            g.dma_start(out=y_out[224:352, 0:65], in_=AB[0:128, 23552:23617]).then_inc(sem, 16)
                            g.dma_start(out=y_out[352:480, 0:512], in_=AB[0:128, 22528:23040]).then_inc(sem, 16)
                        E("gpsimd", adm, reads=mtk + ["qarow"] + khk + qak + ovk + vabk + ROW(0)[1] + PTB(0)[1] + onumk,
                          writes=["yout"], n_inc=176)
            if ODUMP:
                def odump(g, sem):
                    g.dma_start(out=bass.AP(y_out, 0, [[8192, 128], [2048, 4], [1, 2048]]),
                                in_=AB[:, 0:8192]).then_inc(sem, 16)
                    g.dma_start(out=bass.AP(y_out, 128 * 8192, [[8192, 128], [2048, 4], [1, 2048]]),
                                in_=AB[:, 8192:16384]).then_inc(sem, 16)
                E("gpsimd", odump, reads=[("B", g) for g in range(32)], writes=["yout"], n_inc=32)
            allok = [("B", g) for g in range(32)]
            for j in range(8):
                eli = 168 + j
                kick(eli + 1)
                el = stream[eli]
                wob = el["buf"]; wok = el["wk"]
                for tt in range(4):
                    t0 = tt * 512
                    op = 4 + tt % 2

                    def ofn(t, s, wob=wob, t0=t0, op=op):
                        last = None
                        for ci in range(8):
                            last = t.matmul(ps[op][:], lhsT=wob[:, ci * 128:(ci + 1) * 128],
                                            rhs=AB[:, ci * T + t0:ci * T + t0 + 512],
                                            start=(ci == 0), stop=(ci == 7))
                        return last
                    E("tensor", ofn, reads=wok + allok, writes=[f"p{op}"])
                    E("vector", lambda v, s, j=j, t0=t0, tt=tt, op=op: v.tensor_tensor(
                        xv(j, t0, 512), ps[op][:], xv(j, t0, 512), ALU.add),
                      reads=[f"p{op}", ("x", j, tt)], writes=[("x", j, tt)])

        init()
        kick(1)
        ffn_pass(0)
        if stages >= 2:
            attention()
        if stages >= 3:
            ffn_pass(176)

        if DEBUG_DUMP is None and not ADEBUG and not ODUMP:
            def outs(sy, sem):
                for cc in range(8):
                    sy.dma_start(out=y_out[cc * 128:(cc + 1) * 128, :], in_=xv(cc, 0, T)).then_inc(sem, 16)
            E("sync", outs, reads=[("x", cc, tt) for cc in range(8) for tt in range(4)],
              writes=["yout"], n_inc=128)
        E("sync", lambda sy, s: None, reads=["yout"], n_inc=0)

        _LASTPLAN[0] = pl
        with nc.Block() as block:
            def mk(engname):
                def run(eng):
                    mysem = sems[ESEM[engname]]
                    for fn, waits, n_inc, my in pl.ops[engname]:
                        for s, c in waits:
                            eng.wait_ge(sems[s], c)
                        r = fn(eng, mysem)
                        if n_inc and r is not None:
                            r.then_inc(mysem, n_inc)
                return run
            block.tensor(mk("tensor"))
            block.scalar(mk("scalar"))
            block.vector(mk("vector"))
            block.gpsimd(mk("gpsimd"))
            block.sync(mk("sync"))
    return nc


def kernel(**inputs):
    shared, xcores = host_prep(inputs)
    nc = build_nc(STAGES)
    in_maps = [dict(shared, x_in=xcores[c]) for c in range(NCORES)]
    res = run_bass_kernel_spmd(nc, in_maps, core_ids=list(range(NCORES)))
    toks = np.concatenate([np.asarray(res.results[c]["y"]).T for c in range(NCORES)], axis=0)
    return np.ascontiguousarray(toks.reshape(4, 4096, 1024)).astype(np.float32)
